# version 08
# baseline (speedup 1.0000x reference)
import numpy as np
from contextlib import ExitStack
import concourse.bass as bass
import concourse.mybir as mybir
from concourse.bass_utils import run_bass_kernel_spmd
from concourse.alu_op_type import AluOpType as OP

F32 = mybir.dt.float32
BF16 = mybir.dt.bfloat16
I32 = mybir.dt.int32
AF = mybir.ActivationFunctionType
AX = mybir.AxisListType.X

T = 2048
CD = 2048
FI = 8192
NQ = 16
NKV = 4
NT = T // 128
NCC = CD // 128
NIC = FI // 128
SC = 1.0 / np.sqrt(128.0)


class KB:
    ENGINES = ("sync", "gpsimd", "vector", "scalar", "tensor")

    def __init__(self):
        self.nc = bass.Bass("TRN2", target_bir_lowering=False)
        self.es = ExitStack()
        self.ops = {e: [] for e in self.ENGINES}
        self.sems = {}
        self.cnt = {}

    def sem(self, name):
        if name not in self.sems:
            self.sems[name] = self.es.enter_context(self.nc.semaphore(name))
            self.cnt[name] = 0
        return self.sems[name]

    def n(self, name):
        self.sem(name)
        return self.cnt[name]

    def mk(self, eng):
        def emit(meth, *args, w=(), i=None, **kw):
            waits = [(self.sem(s), v) for (s, v) in w if v > 0]
            ih = None
            amt = 0
            if i is not None:
                ih = self.sem(i[0])
                self.cnt[i[0]] += i[1]
                amt = i[1]
            def f(e, meth=meth, args=args, kw=kw, waits=waits, ih=ih, amt=amt):
                for s, v in waits:
                    e.wait_ge(s, v)
                r = getattr(e, meth)(*args, **kw)
                if ih is not None:
                    r.then_inc(ih, amt)
            self.ops[eng].append(f)
        return emit

    def sb(self, name, shape, dt, stack=None, side=None):
        return (stack or self.es).enter_context(
            self.nc.sbuf_tensor(name, shape, dt, side=side))

    def ps(self, name, shape, stack=None):
        return (stack or self.es).enter_context(self.nc.psum_tensor(name, shape, F32))


def build_program(phases=6):
    kb = KB()
    nc = kb.nc
    TE, VE, SE, GQ, SQ = (kb.mk("tensor"), kb.mk("vector"), kb.mk("scalar"),
                          kb.mk("gpsimd"), kb.mk("sync"))

    def din(name, shape, dt):
        return nc.dram_tensor(name, shape, dt, kind="ExternalInput").ap()

    x_d = din("x", [T, CD], F32)
    wq_d = din("wq", [CD, CD], F32)
    wk_d = din("wk", [NKV * 128, CD], F32)
    wv_d = din("wv", [NKV * 128, CD], F32)
    wp_d = din("wp", [CD, CD], F32)
    gw_d = din("gw", [FI, CD], I32)
    uw_d = din("uw", [FI, CD], I32)
    dw_d = din("dw", [CD, FI], I32)
    gs_d = din("gs", [128, 64], F32)
    us_d = din("us", [128, 64], F32)
    ds_d = din("ds", [128, 16], F32)
    cosd = din("cost", [128, T], F32)
    sind = din("sint", [128, T], F32)
    m01_d = din("m01", [128, 128], BF16)

    out_d = nc.dram_tensor("out", [T, CD], F32, kind="ExternalOutput").ap()

    xn_scr = nc.dram_tensor("xn_scr", [T, CD], BF16, kind="Internal").ap()
    qT_scr = nc.dram_tensor("qT_scr", [CD, T], BF16, kind="Internal").ap()
    h_scr = nc.dram_tensor("h_scr", [FI, T], BF16, kind="Internal").ap()
    s_scr = nc.dram_tensor("s_scr", [T, CD], BF16, kind="Internal").ap()
    wg_s = nc.dram_tensor("wg_s", [FI, CD], BF16, kind="Internal").ap()
    wu_s = nc.dram_tensor("wu_s", [FI, CD], BF16, kind="Internal").ap()
    wd_s = nc.dram_tensor("wd_s", [CD, FI], BF16, kind="Internal").ap()
    wq_s = nc.dram_tensor("wq_s", [CD, CD], BF16, kind="Internal").ap()
    wk_s = nc.dram_tensor("wk_s", [NKV * 128, CD], BF16, kind="Internal").ap()
    wv_s = nc.dram_tensor("wv_s", [NKV * 128, CD], BF16, kind="Internal").ap()
    wp_s = nc.dram_tensor("wp_s", [CD, CD], BF16, kind="Internal").ap()

    m01_bf = kb.sb("m01_bf", [128, 128], BF16)
    ones_bf = kb.sb("ones_bf", [128, 128], BF16)
    gs_col = kb.sb("gs_col", [128, 64], F32)
    us_col = kb.sb("us_col", [128, 64], F32)
    ds_col = kb.sb("ds_col", [128, 16], F32)
    ds_negb = kb.sb("ds_negb", [128, 16], F32)
    eps_c = kb.sb("eps_c", [128, 1], F32)
    ptab = ExitStack()
    cos_b = kb.sb("cos_b", [128, T], F32, ptab)
    sin_b = kb.sb("sin_b", [128, T], F32, ptab)

    pgu = ExitStack()
    gu_ring = [kb.sb(f"gu_{r}", [128, 16 * 512], BF16, pgu, side="right")
               for r in range(4)]

    GQ("dma_start", m01_bf[:], m01_d, i=("c_ld", 16))
    GQ("dma_start", cos_b[:], cosd, i=("c_ld", 16))
    GQ("dma_start", sin_b[:], sind, i=("c_ld", 16))
    GQ("dma_start", gs_col[:], gs_d, i=("c_ld", 16))
    GQ("dma_start", us_col[:], us_d, i=("c_ld", 16))
    GQ("dma_start", ds_col[:], ds_d, i=("c_ld", 16))
    C_LD = kb.n("c_ld")
    VE("memset", ones_bf[:], 1.0, i=("c_ve", 1))
    VE("memset", eps_c[:], 1e-5, i=("c_ve", 1))
    VE("tensor_scalar_mul", ds_negb[:], ds_col[:], -128.0,
       w=[("c_ld", C_LD)], i=("c_ve", 1))

    def emit_gu_cast(ch, w=()):
        r0 = ch * 1024
        GQ("dma_start", wg_s[r0:r0 + 1024, :], gw_d[r0:r0 + 1024, :],
           w=w, i=(f"stgu{ch}", 16))
        GQ("dma_start", wu_s[r0:r0 + 1024, :], uw_d[r0:r0 + 1024, :],
           i=(f"stgu{ch}", 16))

    emit_gu_cast(0)
    for ch in range(1, 4):
        emit_gu_cast(ch, w=[("a_c1", min(4 * ch, 14))])

    pxn = ExitStack()
    xnT = kb.sb("xnT", [128, NCC * T], BF16, pxn)
    pa = ExitStack()
    x_ring = [kb.sb(f"x_r{p}", [128, CD], F32, pa) for p in range(4)]
    xnb_ring = [kb.sb(f"xnb_r{p}", [128, CD], BF16, pa) for p in range(4)]
    dump_sb = kb.sb("dump_sb", [128, CD], BF16, pa)
    c1_ring = [kb.sb(f"lc1_{p}", [128, 1], F32, pa) for p in range(4)]
    cm_ring = [kb.sb(f"lcm_{p}", [128, 1], F32, pa) for p in range(4)]
    cs_ring = [kb.sb(f"lcs_{p}", [128, 1], F32, pa) for p in range(4)]
    cd_ring = [kb.sb(f"lcd_{p}", [128, 1], F32, pa) for p in range(4)]
    cr_ring = [kb.sb(f"lcr_{p}", [128, 1], F32, pa) for p in range(4)]

    mb_ring = [kb.sb(f"lmb_{p}", [128, 1], F32, pa) for p in range(4)]

    def a_load(tt):
        p = tt % 4
        SQ("dma_start", x_ring[p][:], x_d[tt * 128:(tt + 1) * 128, :],
           w=[("a_xn", tt - 3)], i=(f"axld{p}", 16))

    def a_head(tt):
        p = tt % 4
        VE("reduce_sum", c1_ring[p][:], x_ring[p][:], axis=AX,
           w=[(f"axld{p}", 16 * (tt // 4 + 1)), ("a_m", tt - 3)], i=("a_c1", 1))
        VE("tensor_scalar_mul", cm_ring[p][:], c1_ring[p][:], 1.0 / CD,
           w=[("a_c1", tt + 1), ("a_ss", tt - 3), ("a_mb", tt - 3)],
           i=("a_m", 1))
        VE("scalar_tensor_tensor", dump_sb[:], x_ring[p][:],
           cm_ring[p][:], x_ring[p][:], OP.subtract, OP.mult,
           accum_out=cs_ring[p][:],
           w=[("a_m", tt + 1), ("a_ss", tt), ("a_std", tt - 3)],
           i=("a_ss", 1))

    def a_sqrt(tt):
        p = tt % 4
        SE("activation", cd_ring[p][:], cs_ring[p][:], AF.Sqrt,
           bias=eps_c[:], scale=1.0 / CD,
           w=([("c_ve", 2)] if tt < 4 else []) + [("a_ss", tt + 1),
                                                  ("a_rc", tt - 3)],
           i=("a_std", 1))

    def a_tail(tt):
        p = tt % 4
        VE("reciprocal", cr_ring[p][:], cd_ring[p][:],
           w=[("a_std", tt + 1), ("a_xn", tt - 3), ("a_mb", tt - 3)],
           i=("a_rc", 1))
        VE("tensor_scalar", mb_ring[p][:], cm_ring[p][:],
           cr_ring[p][:], -1.0, OP.mult, OP.mult,
           w=[("a_rc", tt + 1), ("a_xn", tt - 3)], i=("a_mb", 1))

    def a_xn(tt):
        p = tt % 4
        SE("activation", xnb_ring[p][:], x_ring[p][:], AF.Identity,
           bias=mb_ring[p][:], scale=cr_ring[p][:],
           w=[("a_mb", tt + 1), (f"axnst{p}", 16 * (tt // 4))], i=("a_xn", 1))
        SE("dma_start", out=xn_scr[tt * 128:(tt + 1) * 128, :],
           in_=xnb_ring[p][:], w=[("a_xn", tt + 1)], i=(f"axnst{p}", 16))

    for tt in range(NT):
        a_load(tt)
    a_head(0)
    a_head(1)
    for tt in range(NT):
        a_sqrt(tt)
        a_tail(tt)
        a_xn(tt)
        if tt + 2 < NT:
            a_head(tt + 2)
    AB = [(f"axnst{p}", 64) for p in range(4)]
    for cc in range(NCC):
        SQ("dma_start", xnT[:, cc * T:(cc + 1) * T],
           xn_scr[:, cc * 128:(cc + 1) * 128],
           w=(AB if cc == 0 else []),
           i=("a_xnT", 16), transpose=True)
    A_XNT = kb.n("a_xnT")
    pa.close()

    if phases < 2:
        _tail(kb, [("a_xnT", A_XNT)])
        pxn.close()
        return _finish(kb, nc)

    pb1 = ExitStack()
    silu_ring = [kb.sb(f"sil_{r}", [128, CD], BF16, pb1) for r in range(2)]
    h_ring = [kb.sb(f"h_{r}", [128, CD], BF16, pb1) for r in range(2)]
    dwi_ring = [kb.sb(f"dwi_{r}", [128, 2048], I32, pb1) for r in range(2)]
    dwb_ring = [kb.sb(f"dwb_{r}", [128, 2048], BF16, pb1) for r in range(2)]
    g_ps = kb.ps("g_ps", [128, CD], pb1)
    u_ps = kb.ps("u_ps", [128, CD], pb1)

    A_DONE = [("a_xnT", A_XNT)]

    def emit_slab_fill(k):
        sg, m = k // 2, k % 2
        r = k % 4
        wsrc = wg_s if m == 0 else wu_s
        for cc in range(NCC):
            w = []
            if cc == 0:
                w.append((f"stgu{sg // 2}", 32))
                if k >= 4:
                    w.append(("b1_gmm" if m == 0 else "b1_umm", 4 * sg - 4))
            SQ("dma_start", gu_ring[r][:, cc * 512:(cc + 1) * 512],
               wsrc[sg * 512:(sg + 1) * 512, cc * 128:(cc + 1) * 128],
               w=w, i=(f"guls{r}", 16), transpose=True)

    def emit_dw_load(idx):
        if idx >= 64:
            return
        oc, qq = idx // 4, idx % 4
        p = idx % 2
        w = []
        if idx < 2:
            w = list(A_DONE)
        else:
            w = [("b1_wddq", idx - 1)]
        SQ("dma_start", dwi_ring[p][:],
           dw_d[oc * 128:(oc + 1) * 128, qq * 2048:(qq + 1) * 2048],
           w=w, i=(f"dwld{p}", 16))

    def emit_dw_compute(idx):
        oc = idx // 4
        p = idx % 2
        SE("activation", dwb_ring[p][:], dwi_ring[p][:], AF.Identity,
           bias=ds_negb[:, oc:oc + 1], scale=ds_col[:, oc:oc + 1],
           w=[(f"dwld{p}", 16 * (idx // 2 + 1)), ("c_ve", 3),
              (f"wdst{p}", 16 * (idx // 2))],
           i=("b1_wddq", 1))
        oc2, qq2 = idx // 4, idx % 4
        GQ("dma_start",
           wd_s[oc2 * 128:(oc2 + 1) * 128, qq2 * 2048:(qq2 + 1) * 2048],
           dwb_ring[p][:], w=[("b1_wddq", idx + 1)], i=(f"wdst{p}", 16))

    def emit_attn_casts():
        for q in range(4):
            r0 = q * 512
            GQ("dma_start", wq_s[r0:r0 + 512, :], wq_d[r0:r0 + 512, :],
               i=("stgq", 16))
            GQ("dma_start", wp_s[r0:r0 + 512, :], wp_d[r0:r0 + 512, :],
               i=("stgp", 16))
        GQ("dma_start", wk_s, wk_d, i=("stgk", 16))
        GQ("dma_start", wv_s, wv_d, i=("stgv", 16))

    emit_slab_fill(0)
    emit_slab_fill(1)
    emit_dw_load(0)
    emit_dw_load(1)

    for ic in range(NIC):
        sg = ic // 4
        io = ic % 4
        if io == 0:
            k = 2 * sg
            if sg + 1 < 16:
                emit_slab_fill(k + 2)
                emit_slab_fill(k + 3)
            if sg in (2, 4, 6, 8):
                emit_gu_cast(4 + (sg - 2) // 2)
            if sg == 10:
                emit_attn_casts()
        gslab = gu_ring[(2 * sg) % 4]
        uslab = gu_ring[(2 * sg + 1) % 4]
        gfills = sg + 1
        w = [(f"guls{(2 * sg) % 4}", 16 * 16 * ((2 * sg) // 4 + 1)),
             ("b1_sil", ic)]
        if ic == 0:
            w += A_DONE
        first = True
        for cc in range(NCC):
            for jc in range(4):
                TE("matmul", g_ps[:, jc * 512:(jc + 1) * 512],
                   gslab[:, cc * 512 + io * 128: cc * 512 + io * 128 + 128],
                   xnT[:, cc * T + jc * 512: cc * T + (jc + 1) * 512],
                   start=(cc == 0), stop=(cc == NCC - 1),
                   w=(w if first else ()),
                   i=(("b1_gmm", 1) if (cc == NCC - 1 and jc == 3) else None))
                first = False
        w = [(f"guls{(2 * sg + 1) % 4}", 16 * 16 * ((2 * sg + 1) // 4 + 1)),
             ("b1_h", ic)]
        if ic == 0:
            w += A_DONE
        first = True
        for cc in range(NCC):
            for jc in range(4):
                TE("matmul", u_ps[:, jc * 512:(jc + 1) * 512],
                   uslab[:, cc * 512 + io * 128: cc * 512 + io * 128 + 128],
                   xnT[:, cc * T + jc * 512: cc * T + (jc + 1) * 512],
                   start=(cc == 0), stop=(cc == NCC - 1),
                   w=(w if first else ()),
                   i=(("b1_umm", 1) if (cc == NCC - 1 and jc == 3) else None))
                first = False
        SE("activation", silu_ring[ic % 2][:], g_ps[:], AF.Silu,
           scale=gs_col[:, ic:ic + 1],
           w=[("b1_gmm", ic + 1), ("b1_h", ic - 1)] +
             ([("c_ld", C_LD)] if ic == 0 else []),
           i=("b1_sil", 1))
        emit_dw_compute(ic)
        emit_dw_load(ic + 2)
        VE("scalar_tensor_tensor", h_ring[ic % 2][:], u_ps[:],
           us_col[:, ic:ic + 1], silu_ring[ic % 2][:], OP.mult, OP.mult,
           w=[("b1_umm", ic + 1), ("b1_sil", ic + 1),
              (f"hst{ic % 2}", 16 * (ic // 2))] +
             ([("c_ld", C_LD)] if ic == 0 else []),
           i=("b1_h", 1))
        GQ("dma_start", h_scr[ic * 128:(ic + 1) * 128, :], h_ring[ic % 2][:],
           w=[("b1_h", ic + 1)], i=(f"hst{ic % 2}", 16))
    pb1.close()

    B1_DONE = [("hst0", 512), ("hst1", 512), ("wdst0", 512), ("wdst1", 512)]

    if phases < 3:
        _tail(kb, B1_DONE)
        pxn.close()
        return _finish(kb, nc)

    pgu.close()
    pkv = ExitStack()
    KT = kb.sb("KT", [128, NKV * T], BF16, pkv, side="right")
    V_sb = kb.sb("V_sb", [128, NT * 512], BF16, pkv, side="right")
    pc1 = ExitStack()
    wsl_ring = [kb.sb(f"wsl_{r}", [128, 16 * 512], BF16, pc1) for r in range(2)]
    tmpA = kb.sb("tmpA", [128, T], F32, pc1)
    tmpB = kb.sb("tmpB", [128, T], F32, pc1)
    qst_ring = [kb.sb(f"qst_{r}", [128, T], BF16, pc1) for r in range(2)]
    kq_ps = [kb.ps(f"kq_ps{r}", [128, CD], pc1) for r in range(2)]

    def c1_fill(slot, wsrc, rows0, extra_w):
        for cc in range(NCC):
            w = list(extra_w) if cc == 0 else []
            SQ("dma_start", wsl_ring[slot][:, cc * 512:(cc + 1) * 512],
               wsrc[rows0:rows0 + 512, cc * 128:(cc + 1) * 128],
               w=w, i=(f"wsl{slot}", 16), transpose=True)

    c1_fill(0, wk_s, 0, B1_DONE + [("stgk", 16)])
    c1_fill(1, wv_s, 0, B1_DONE + [("stgv", 16)])

    def rope(ps, dst_lo, dst_hi, ps_w, dst_w):
        base = kb.n("ro")
        w0 = list(ps_w) + [("c_ld", C_LD), ("ro", base)]
        VE("tensor_tensor", tmpA[0:64, :], ps[0:64, :], cos_b[0:64, :],
           OP.mult, w=w0, i=("ro", 1))
        VE("tensor_tensor", tmpB[0:64, :], ps[64:128, :], sin_b[64:128, :],
           OP.mult, w=w0, i=("ro", 1))
        VE("tensor_tensor", tmpA[64:128, :], ps[64:128, :], cos_b[64:128, :],
           OP.mult, w=w0, i=("ro", 1))
        VE("tensor_tensor", tmpB[64:128, :], ps[0:64, :], sin_b[0:64, :],
           OP.mult, w=w0, i=("ro", 1))
        wa = [("ro", base + 4)] + list(dst_w)
        VE("tensor_tensor", dst_lo, tmpA[0:64, :], tmpB[0:64, :], OP.subtract,
           w=wa, i=("ro", 1))
        VE("tensor_tensor", dst_hi, tmpA[64:128, :], tmpB[64:128, :], OP.add,
           w=wa, i=("ro", 1))

    for hk in range(NKV):
        p = hk % 2
        w = [("wsl0", 256)]
        if hk >= 2:
            w.append(("ro", 6 * (hk - 1)))
        first = True
        for cc in range(NCC):
            for jc in range(4):
                TE("matmul", kq_ps[p][:, jc * 512:(jc + 1) * 512],
                   wsl_ring[0][:, cc * 512 + hk * 128: cc * 512 + hk * 128 + 128],
                   xnT[:, cc * T + jc * 512: cc * T + (jc + 1) * 512],
                   start=(cc == 0), stop=(cc == NCC - 1),
                   w=(w if first else ()),
                   i=(("c1_kmm", 1) if (cc == NCC - 1 and jc == 3) else None))
                first = False
        rope(kq_ps[p], KT[0:64, hk * T:(hk + 1) * T],
             KT[64:128, hk * T:(hk + 1) * T], [("c1_kmm", hk + 1)], [])

    for tt in range(NT):
        p = tt % 2
        if tt == 0:
            w = [("wsl1", 256), ("ro", 18)]
        elif tt == 1:
            w = [("ro", 24)]
        else:
            w = [("c1_vev", tt - 1)]
        first = True
        for cc in range(NCC):
            TE("matmul", kq_ps[p][:, 0:512],
               xnT[:, cc * T + tt * 128: cc * T + tt * 128 + 128],
               wsl_ring[1][:, cc * 512:(cc + 1) * 512],
               start=(cc == 0), stop=(cc == NCC - 1),
               w=(w if first else ()),
               i=(("c1_vmm", 1) if cc == NCC - 1 else None))
            first = False
        SE("activation", V_sb[:, tt * 512:(tt + 1) * 512], kq_ps[p][:, 0:512],
           AF.Identity, w=[("c1_vmm", tt + 1)], i=("c1_vev", 1))

    for osl in range(4):
        slot = osl % 2
        if osl == 0:
            ew = [("stgq", 64), ("c1_kmm", 4)]
        elif osl == 1:
            ew = [("c1_vmm", 16)]
        else:
            ew = [("c1_qmm", 4 * (osl - 2) + 4)]
        c1_fill(slot, wq_s, osl * 512, ew)
        for hh in range(4):
            hq = osl * 4 + hh
            p = hq % 2
            w = [(f"wsl{slot}", 16 * 16 * (osl // 2 + 2))]
            if hq == 0:
                w.append(("c1_vev", 15))
            elif hq == 1:
                w.append(("c1_vev", 16))
            else:
                w.append(("ro", 6 * (hq + 3)))
            first = True
            for cc in range(NCC):
                for jc in range(4):
                    TE("matmul", kq_ps[p][:, jc * 512:(jc + 1) * 512],
                       wsl_ring[slot][:, cc * 512 + hh * 128:
                                      cc * 512 + hh * 128 + 128],
                       xnT[:, cc * T + jc * 512: cc * T + (jc + 1) * 512],
                       start=(cc == 0), stop=(cc == NCC - 1),
                       w=(w if first else ()),
                       i=(("c1_qmm", 1) if (cc == NCC - 1 and jc == 3) else None))
                    first = False
            rope(kq_ps[p], qst_ring[hq % 2][0:64, :], qst_ring[hq % 2][64:128, :],
                 [("c1_qmm", hq + 1)],
                 [(f"qstst{hq % 2}", 16 * (hq // 2))])
            GQ("dma_start", qT_scr[hq * 128:(hq + 1) * 128, :],
               qst_ring[hq % 2][:], w=[("ro", 6 * (hq + 5))],
               i=(f"qstst{hq % 2}", 16))
    pc1.close()
    pxn.close()
    ptab.close()

    C1_DONE = [("qstst0", 128), ("qstst1", 128)]

    if phases < 4:
        _tail(kb, C1_DONE)
        pkv.close()
        return _finish(kb, nc)

    pyt = ExitStack()
    YT = kb.sb("YT", [128, NQ * T], BF16, pyt)
    wp_sl = [kb.sb(f"wp_{q}", [128, 16 * 512], BF16, pyt) for q in range(4)]
    pc2 = ExitStack()
    qr_ring = [kb.sb(f"qr_{r}", [128, T], BF16, pc2) for r in range(3)]
    E_ring = [kb.sb(f"E_{r}", [128, 512], BF16, pc2) for r in range(4)]
    et_ring = [kb.sb(f"et_{r}", [128, 128], BF16, pc2) for r in range(2)]
    rc_ring = [kb.sb(f"rc_{r}", [128, 512], F32, pc2) for r in range(2)]
    s_ps = [kb.ps(f"s_ps{r}", [128, 512], pc2) for r in range(3)]
    d_ps = [kb.ps(f"d_ps{r}", [128, 512], pc2) for r in range(2)]
    o_ps = [kb.ps(f"o_ps{r}", [128, 512], pc2) for r in range(2)]

    groups = []
    njc_of = []
    njc = 0
    for hq in range(NQ):
        for jc in range(4):
            njs = 4 * jc + 4
            for js in range(njs):
                groups.append((hq, jc, js, njc, njs))
            njc += 1
    NG = len(groups)
    LAG = 2

    smm_after = {}
    exp_done_at = {}
    et_last_use = [0, 0]

    def c2_emit_s(j):
        hq, jc, js, njcj, njs = groups[j]
        kv = hq // 4
        if jc == 0 and js == 0:
            w = list(C1_DONE) if hq == 0 else []
            if hq >= 3:
                w.append(("c2_smm", 40 * (hq - 2)))
            SQ("dma_start", qr_ring[hq % 3][:],
               qT_scr[hq * 128:(hq + 1) * 128, :],
               w=w, i=(f"qrld{hq % 3}", 16))
            if 4 <= hq <= 7:
                ph = hq - 4
                for hd in range(NCC):
                    w2 = []
                    if ph == 0 and hd == 0:
                        w2 = C1_DONE + [("stgp", 64)]
                    SQ("dma_start", wp_sl[ph][:, hd * 512:(hd + 1) * 512],
                       wp_s[ph * 512:(ph + 1) * 512, hd * 128:(hd + 1) * 128],
                       w=w2, i=(f"wpsl{ph}", 16), transpose=True)
        w = [(f"qrld{hq % 3}", 16 * (hq // 3 + 1))]
        if j >= 3:
            pe = exp_done_at[j - 3]
            w.append(("c2_eA", pe[0]))
            if pe[1]:
                w.append(("c2_eV", pe[1]))
        if j == 0:
            w += [("ro", 24), ("c1_vev", NT)]
        TE("matmul", s_ps[j % 3][:],
           KT[:, kv * T + js * 128: kv * T + js * 128 + 128],
           qr_ring[hq % 3][:, jc * 512:(jc + 1) * 512],
           start=True, stop=True, w=w, i=("c2_smm", 1))
        smm_after[j] = kb.n("c2_smm")

    def c2_emit_exp(j):
        hq, jc, js, njcj, njs = groups[j]
        sp = j % 3
        r = js - 4 * jc
        eslot = E_ring[j % 4]
        we = [("c2_smm", smm_after[j]), ("c2_eo", j - 3), ("c2_dA", j - 3)]
        if j == 0:
            we.append(("c_ld", C_LD))
        if r < 0:
            SE("activation", eslot[:], s_ps[sp][:], AF.Exp,
               scale=SC, w=we, i=("c2_eA", 1))
        else:
            if r > 0:
                VE("memset", eslot[:, 0:128 * r], 0.0,
                   w=we, i=("c2_eV", 1))
            SE("activation", et_ring[j % 2][:],
               s_ps[sp][:, 128 * r:128 * r + 128], AF.Exp, scale=SC,
               w=we + ([("c2_eV", et_last_use[j % 2])]
                       if et_last_use[j % 2] else []),
               i=("c2_eA", 1))
            if r < 3:
                SE("activation", eslot[:, 128 * r + 128:512],
                   s_ps[sp][:, 128 * r + 128:512], AF.Exp, scale=SC,
                   w=we, i=("c2_eA", 1))
            VE("tensor_tensor", eslot[:, 128 * r:128 * r + 128],
               et_ring[j % 2][:], m01_bf[:], OP.mult,
               w=[("c2_eA", kb.n("c2_eA")), ("c_ld", C_LD)],
               i=("c2_eV", 1))
            et_last_use[j % 2] = kb.n("c2_eV")
        exp_done_at[j] = (kb.n("c2_eA"), kb.n("c2_eV"))

    def c2_emit_do(k):
        hq, jc, js, njck, njs = groups[k]
        kv = hq // 4
        par = njck % 2
        eslot = E_ring[k % 4]
        pe = exp_done_at[k]
        w = [("c2_eA", pe[0]), ("c_ve", 1)]
        if pe[1]:
            w.append(("c2_eV", pe[1]))
        if js == 0:
            w.append(("c2_yev", njck - 1))
        TE("matmul", d_ps[par][:], ones_bf[:], eslot[:],
           start=(js == 0), stop=(js == njs - 1), w=w, i=("c2_dA", 1))
        w = [("c2_eA", pe[0])]
        if pe[1]:
            w.append(("c2_eV", pe[1]))
        if js == 0:
            w.append(("c2_yev", njck - 1))
        TE("matmul", o_ps[par][:],
           V_sb[:, js * 512 + kv * 128: js * 512 + kv * 128 + 128],
           eslot[:], start=(js == 0), stop=(js == njs - 1), w=w,
           i=("c2_eo", 1))
        if js == njs - 1:
            VE("reciprocal", rc_ring[par][:], d_ps[par][:],
               w=[("c2_dA", kb.n("c2_dA")), ("c2_yev", njck - 1)],
               i=("c2_rc", 1))
            VE("tensor_tensor",
               YT[:, hq * T + jc * 512: hq * T + (jc + 1) * 512],
               o_ps[par][:], rc_ring[par][:], OP.mult,
               w=[("c2_eo", kb.n("c2_eo")), ("c2_rc", njck + 1)],
               i=("c2_yev", 1))

    for j in range(LAG):
        c2_emit_s(j)
        c2_emit_exp(j)
    for k in range(NG):
        if k + LAG < NG:
            c2_emit_s(k + LAG)
            c2_emit_exp(k + LAG)
        c2_emit_do(k)

    pc2.close()
    pkv.close()

    C2_DONE = [("c2_yev", 64)]

    if phases < 5:
        _tail(kb, C2_DONE + [(f"wpsl{q}", 256) for q in range(4)])
        pyt.close()
        return _finish(kb, nc)

    pc3 = ExitStack()
    sst_ring = [kb.sb(f"sst_{r}", [128, 8 * 512], BF16, pc3) for r in range(2)]
    c3_ps = kb.ps("c3_ps", [128, 4096], pc3)

    ph_idx = 0
    for oct_ in range(2):
        for ph in range(4):
            pr = ph_idx % 2
            w = [(f"wpsl{ph}", 256), ("c3_ev", ph_idx)]
            if ph_idx == 0:
                w += C2_DONE
            first = True
            for hd in range(NCC):
                for tt8 in range(8):
                    ttile = oct_ * 8 + tt8
                    TE("matmul", c3_ps[:, tt8 * 512:(tt8 + 1) * 512],
                       YT[:, hd * T + ttile * 128: hd * T + ttile * 128 + 128],
                       wp_sl[ph][:, hd * 512:(hd + 1) * 512],
                       start=(hd == 0), stop=(hd == NCC - 1),
                       w=(w if first else ()),
                       i=(("c3_mm", 1) if (hd == NCC - 1 and tt8 == 7) else None))
                    first = False
            VE("tensor_copy", sst_ring[pr][:], c3_ps[:],
               w=[("c3_mm", ph_idx + 1), (f"c3sst{pr}", 16 * (ph_idx // 2))] +
                 (C2_DONE if ph_idx < 2 else []),
               i=("c3_ev", 1))
            dst = s_scr[oct_ * 1024:(oct_ + 1) * 1024,
                        ph * 512:(ph + 1) * 512].rearrange(
                            "(tt p) c -> p tt c", p=128)
            src = sst_ring[pr][:].rearrange("p (tt c) -> p tt c", tt=8)
            GQ("dma_start", dst, src, w=[("c3_ev", ph_idx + 1)],
               i=(f"c3sst{pr}", 16))
            ph_idx += 1
    pc3.close()
    pyt.close()

    C3_DONE = [("c3sst0", 64), ("c3sst1", 64)]

    if phases < 6:
        _tail(kb, C3_DONE)
        return _finish(kb, nc)

    pb2 = ExitStack()
    hT = kb.sb("hT", [128, NIC * 1024], BF16, pb2)
    wd_ring = [kb.sb(f"wdr_{r}", [128, 512], BF16, pb2) for r in range(8)]
    sld_ring = [kb.sb(f"sld_{r}", [128, 8 * 512], BF16, pb2) for r in range(2)]
    x_ring2 = [kb.sb(f"x2_{r}", [128, 512], F32, pb2) for r in range(4)]
    g_ring = [kb.sb(f"g2_{r}", [128, 512], BF16, pb2) for r in range(4)]
    sg_ring = [kb.sb(f"sg_{r}", [128, 512], F32, pb2) for r in range(4)]
    o_ring = [kb.sb(f"o2_{r}", [128, 512], F32, pb2) for r in range(4)]
    b2_ps = kb.ps("b2_ps", [128, 4096], pb2)

    sl_idx = 0
    ph_idx = 0
    for oct_ in range(2):
        for ic in range(NIC):
            w = []
            if ic == 0:
                if oct_ == 0:
                    w = B1_DONE + C3_DONE
                else:
                    w = [("b2_mm8", 256)]
            SQ("dma_start", hT[:, ic * 1024:(ic + 1) * 1024],
               h_scr[ic * 128:(ic + 1) * 128, oct_ * 1024:(oct_ + 1) * 1024],
               w=w, i=("b2hld", 16))
        for ph in range(4):
            pr = ph_idx % 2
            w = [("b2_comb", 8 * (ph_idx - 1))]
            if ph_idx == 0:
                w += C3_DONE
            sdst = sld_ring[pr][:].rearrange("p (tt c) -> p tt c", tt=8)
            ssrc = s_scr[oct_ * 1024:(oct_ + 1) * 1024,
                         ph * 512:(ph + 1) * 512].rearrange(
                             "(tt p) c -> p tt c", p=128)
            SQ("dma_start", sdst, ssrc, w=w, i=(f"sldld{pr}", 16))
            for tt8 in range(8):
                ttile = oct_ * 8 + tt8
                k = ph_idx * 8 + tt8
                GQ("dma_start", x_ring2[k % 4][:],
                   x_d[ttile * 128:(ttile + 1) * 128, ph * 512:(ph + 1) * 512],
                   w=[("b2_comb", k - 3)], i=(f"xld2{k % 4}", 16))
            for ic in range(NIC):
                w = []
                if sl_idx == 0:
                    w = list(C3_DONE) + [("wdst0", 512), ("wdst1", 512)]
                if sl_idx >= 8:
                    w.append(("b2_mm8", sl_idx - 7))
                SQ("dma_start", wd_ring[sl_idx % 8][:],
                   wd_s[ph * 512:(ph + 1) * 512, ic * 128:(ic + 1) * 128],
                   w=w, i=(f"wdsl{sl_idx % 8}", 16), transpose=True)
                sl_idx += 1
            base_sl = ph_idx * NIC
            for ic in range(NIC):
                w = [(f"wdsl{(base_sl + ic) % 8}",
                      16 * ((base_sl + ic) // 8 + 1))]
                if ic == 0:
                    w += [("b2hld", 1024 * (oct_ + 1)),
                          ("b2_sil", 8 * ph_idx)]
                    if ph_idx == 0:
                        w.append(("c3_ev", 8))
                for tt8 in range(8):
                    TE("matmul", b2_ps[:, tt8 * 512:(tt8 + 1) * 512],
                       hT[:, ic * 1024 + tt8 * 128: ic * 1024 + tt8 * 128 + 128],
                       wd_ring[(base_sl + ic) % 8][:],
                       start=(ic == 0), stop=(ic == NIC - 1),
                       w=(w if tt8 == 0 else ()),
                       i=(("b2_mm8", 1) if tt8 == 7 else None))
                    w = []
            for tt8 in range(8):
                ttile = oct_ * 8 + tt8
                gi = tt8 % 4
                k = ph_idx * 8 + tt8
                SE("activation", g_ring[gi][:],
                   b2_ps[:, tt8 * 512:(tt8 + 1) * 512], AF.Silu,
                   w=[("b2_mm8", 64 * (ph_idx + 1)), ("b2_comb", k - 3)],
                   i=("b2_sil", 1))
                VE("tensor_tensor", sg_ring[gi][:], g_ring[gi][:],
                   sld_ring[pr][:, tt8 * 512:(tt8 + 1) * 512], OP.mult,
                   w=[("b2_sil", k + 1), (f"sldld{pr}", 16 * (ph_idx // 2 + 1))],
                   i=("b2_sg", 1))
                VE("tensor_tensor", o_ring[gi][:], sg_ring[gi][:],
                   x_ring2[gi][:], OP.add,
                   w=[("b2_sg", k + 1), (f"xld2{gi}", 16 * (k // 4 + 1)),
                      (f"outst{gi}", 16 * (k // 4))],
                   i=("b2_comb", 1))
                GQ("dma_start",
                   out_d[ttile * 128:(ttile + 1) * 128, ph * 512:(ph + 1) * 512],
                   o_ring[gi][:], w=[("b2_comb", k + 1)],
                   i=(f"outst{gi}", 16))
            ph_idx += 1
    pb2.close()

    _tail(kb, [(f"outst{r}", 16 * 16) for r in range(4)])
    return _finish(kb, nc)


def _tail(kb, fin):
    fy = [(kb.sem(s), v) for s, v in fin if v > 0]
    kb.ops["gpsimd"].append(lambda e, fy=fy: [e.wait_ge(s, v) for s, v in fy])


def _finish(kb, nc):
    with nc.Block() as block:
        @block.sync
        def _(e):
            for f in kb.ops["sync"]:
                f(e)

        @block.gpsimd
        def _(e):
            for f in kb.ops["gpsimd"]:
                f(e)

        @block.vector
        def _(e):
            for f in kb.ops["vector"]:
                f(e)

        @block.scalar
        def _(e):
            for f in kb.ops["scalar"]:
                f(e)

        @block.tensor
        def _(e):
            for f in kb.ops["tensor"]:
                f(e)

    kb.es.close()
    return kb


def _consts():
    import ml_dtypes
    inv_freq = 1.0 / (10000.0 ** (np.arange(64, dtype=np.float64) / 64.0))
    pos = np.arange(T, dtype=np.float64)
    ang = inv_freq[:, None] * pos[None, :]
    cos = np.concatenate([np.cos(ang), np.cos(ang)], axis=0)
    sin = np.concatenate([np.sin(ang), np.sin(ang)], axis=0)
    cosf = np.ascontiguousarray(cos.astype(np.float32))
    sinf = np.ascontiguousarray(sin.astype(np.float32))
    ii, jj = np.meshgrid(np.arange(128), np.arange(128), indexing="ij")
    m01 = np.ascontiguousarray((jj >= ii).astype(ml_dtypes.bfloat16))
    return cosf, sinf, m01


_NC_CACHE = [None]


def kernel(L0_attn_x, L0_attn_Wq, L0_attn_Wk, L0_attn_Wv, L0_attn_Wp,
           L0_ffn_gate_w, L0_ffn_gate_scale, L0_ffn_up_w, L0_ffn_up_scale,
           L0_ffn_down_w, L0_ffn_down_scale):
    x = np.ascontiguousarray(np.asarray(L0_attn_x, dtype=np.float32))
    cosf, sinf, m01 = _consts()
    common = {
        "wq": np.ascontiguousarray(np.asarray(L0_attn_Wq, np.float32)),
        "wk": np.ascontiguousarray(np.asarray(L0_attn_Wk, np.float32)),
        "wv": np.ascontiguousarray(np.asarray(L0_attn_Wv, np.float32)),
        "wp": np.ascontiguousarray(np.asarray(L0_attn_Wp, np.float32)),
        "gw": np.ascontiguousarray(np.asarray(L0_ffn_gate_w, np.int32)),
        "uw": np.ascontiguousarray(np.asarray(L0_ffn_up_w, np.int32)),
        "dw": np.ascontiguousarray(np.asarray(L0_ffn_down_w, np.int32)),
        "gs": np.ascontiguousarray(
            np.asarray(L0_ffn_gate_scale, np.float32).reshape(64, 128).T),
        "us": np.ascontiguousarray(
            np.asarray(L0_ffn_up_scale, np.float32).reshape(64, 128).T),
        "ds": np.ascontiguousarray(
            np.asarray(L0_ffn_down_scale, np.float32).reshape(16, 128).T),
        "cost": cosf, "sint": sinf, "m01": m01,
    }
    in_maps = [dict(common, x=np.ascontiguousarray(x[b])) for b in range(8)]
    if _NC_CACHE[0] is None:
        _NC_CACHE[0] = build_program()
    nc = _NC_CACHE[0].nc
    res = run_bass_kernel_spmd(nc, in_maps, core_ids=list(range(8)))
    return np.stack([np.asarray(res.results[b]["out"]) for b in range(8)],
                    axis=0)


if __name__ == "__main__":
    build_program()
    print("built ok")


# version 09 (final)
# speedup vs baseline: 1.1878x; 1.1878x over previous
import numpy as np
from contextlib import ExitStack
import concourse.bass as bass
import concourse.mybir as mybir
from concourse.bass_utils import run_bass_kernel_spmd
from concourse.alu_op_type import AluOpType as OP

F32 = mybir.dt.float32
BF16 = mybir.dt.bfloat16
I32 = mybir.dt.int32
AF = mybir.ActivationFunctionType
AX = mybir.AxisListType.X

T = 2048
CD = 2048
FI = 8192
NQ = 16
NKV = 4
NT = T // 128
NCC = CD // 128
NIC = FI // 128
SC = 1.0 / np.sqrt(128.0)


class KB:
    ENGINES = ("sync", "gpsimd", "vector", "scalar", "tensor")

    def __init__(self):
        self.nc = bass.Bass("TRN2", target_bir_lowering=False)
        self.es = ExitStack()
        self.ops = {e: [] for e in self.ENGINES}
        self.sems = {}
        self.cnt = {}

    def sem(self, name):
        if name not in self.sems:
            self.sems[name] = self.es.enter_context(self.nc.semaphore(name))
            self.cnt[name] = 0
        return self.sems[name]

    def n(self, name):
        self.sem(name)
        return self.cnt[name]

    def mk(self, eng):
        def emit(meth, *args, w=(), i=None, **kw):
            waits = [(self.sem(s), v) for (s, v) in w if v > 0]
            ih = None
            amt = 0
            if i is not None:
                ih = self.sem(i[0])
                self.cnt[i[0]] += i[1]
                amt = i[1]
            def f(e, meth=meth, args=args, kw=kw, waits=waits, ih=ih, amt=amt):
                for s, v in waits:
                    e.wait_ge(s, v)
                r = getattr(e, meth)(*args, **kw)
                if ih is not None:
                    r.then_inc(ih, amt)
            self.ops[eng].append(f)
        return emit

    def sb(self, name, shape, dt, stack=None, side=None):
        return (stack or self.es).enter_context(
            self.nc.sbuf_tensor(name, shape, dt, side=side))

    def ps(self, name, shape, stack=None):
        return (stack or self.es).enter_context(self.nc.psum_tensor(name, shape, F32))


def build_program(phases=6):
    kb = KB()
    nc = kb.nc
    TE, VE, SE, GQ, SQ = (kb.mk("tensor"), kb.mk("vector"), kb.mk("scalar"),
                          kb.mk("gpsimd"), kb.mk("sync"))

    def din(name, shape, dt):
        return nc.dram_tensor(name, shape, dt, kind="ExternalInput").ap()

    x_d = din("x", [T, CD], F32)
    wq_d = din("wq", [CD, CD], F32)
    wk_d = din("wk", [NKV * 128, CD], F32)
    wv_d = din("wv", [NKV * 128, CD], F32)
    wp_d = din("wp", [CD, CD], F32)
    gw_d = din("gw", [FI, CD], I32)
    uw_d = din("uw", [FI, CD], I32)
    dw_d = din("dw", [CD, FI], I32)
    gs_d = din("gs", [128, 64], F32)
    us_d = din("us", [128, 64], F32)
    ds_d = din("ds", [128, 16], F32)
    cosd = din("cost", [128, T], F32)
    sind = din("sint", [128, T], F32)
    m01_d = din("m01", [128, 128], BF16)

    out_d = nc.dram_tensor("out", [T, CD], F32, kind="ExternalOutput").ap()

    xn_scr = nc.dram_tensor("xn_scr", [T, CD], BF16, kind="Internal").ap()
    qT_scr = nc.dram_tensor("qT_scr", [CD, T], BF16, kind="Internal").ap()
    h_scr = nc.dram_tensor("h_scr", [FI, T], BF16, kind="Internal").ap()
    s_scr = nc.dram_tensor("s_scr", [T, CD], BF16, kind="Internal").ap()
    wg_s = nc.dram_tensor("wg_s", [FI, CD], BF16, kind="Internal").ap()
    wu_s = nc.dram_tensor("wu_s", [FI, CD], BF16, kind="Internal").ap()
    wd_s = nc.dram_tensor("wd_s", [CD, FI], BF16, kind="Internal").ap()
    wq_s = nc.dram_tensor("wq_s", [CD, CD], BF16, kind="Internal").ap()
    wk_s = nc.dram_tensor("wk_s", [NKV * 128, CD], BF16, kind="Internal").ap()
    wv_s = nc.dram_tensor("wv_s", [NKV * 128, CD], BF16, kind="Internal").ap()
    wp_s = nc.dram_tensor("wp_s", [CD, CD], BF16, kind="Internal").ap()

    m01_bf = kb.sb("m01_bf", [128, 128], BF16)
    ones_bf = kb.sb("ones_bf", [128, 128], BF16)
    gs_col = kb.sb("gs_col", [128, 64], F32)
    us_col = kb.sb("us_col", [128, 64], F32)
    ds_col = kb.sb("ds_col", [128, 16], F32)
    ds_negb = kb.sb("ds_negb", [128, 16], F32)
    eps_c = kb.sb("eps_c", [128, 1], F32)
    ptab = ExitStack()
    cos_b = kb.sb("cos_b", [128, T], F32, ptab)
    sin_b = kb.sb("sin_b", [128, T], F32, ptab)

    pgu = ExitStack()
    gu_ring = [kb.sb(f"gu_{r}", [128, 16 * 512], BF16, pgu, side="right")
               for r in range(4)]

    GQ("dma_start", m01_bf[:], m01_d, i=("c_ld", 16))
    GQ("dma_start", cos_b[:], cosd, i=("c_ld", 16))
    GQ("dma_start", sin_b[:], sind, i=("c_ld", 16))
    GQ("dma_start", gs_col[:], gs_d, i=("c_ld", 16))
    GQ("dma_start", us_col[:], us_d, i=("c_ld", 16))
    GQ("dma_start", ds_col[:], ds_d, i=("c_ld", 16))
    C_LD = kb.n("c_ld")
    VE("memset", ones_bf[:], 1.0, i=("c_ve", 1))
    VE("memset", eps_c[:], 1e-5, i=("c_ve", 1))
    VE("tensor_scalar_mul", ds_negb[:], ds_col[:], -128.0,
       w=[("c_ld", C_LD)], i=("c_ve", 1))

    def emit_gu_cast(ch, w=()):
        r0 = ch * 1024
        GQ("dma_start", wg_s[r0:r0 + 1024, :], gw_d[r0:r0 + 1024, :],
           w=w, i=(f"stgu{ch}", 16))
        GQ("dma_start", wu_s[r0:r0 + 1024, :], uw_d[r0:r0 + 1024, :],
           i=(f"stgu{ch}", 16))

    for ch in range(4):
        emit_gu_cast(ch, w=[("a_c1", min(3 + 3 * ch, 14))] if ch else
                     [("a_c1", 2)])

    pxn = ExitStack()
    xnT = kb.sb("xnT", [128, NCC * T], BF16, pxn)
    pa = ExitStack()
    x_ring = [kb.sb(f"x_r{p}", [128, CD], F32, pa) for p in range(4)]
    xnb_ring = [kb.sb(f"xnb_r{p}", [128, CD], BF16, pa) for p in range(4)]
    dump_sb = kb.sb("dump_sb", [128, CD], BF16, pa)
    c1_ring = [kb.sb(f"lc1_{p}", [128, 1], F32, pa) for p in range(4)]
    cm_ring = [kb.sb(f"lcm_{p}", [128, 1], F32, pa) for p in range(4)]
    cs_ring = [kb.sb(f"lcs_{p}", [128, 1], F32, pa) for p in range(4)]
    cd_ring = [kb.sb(f"lcd_{p}", [128, 1], F32, pa) for p in range(4)]
    cr_ring = [kb.sb(f"lcr_{p}", [128, 1], F32, pa) for p in range(4)]

    mb_ring = [kb.sb(f"lmb_{p}", [128, 1], F32, pa) for p in range(4)]

    def a_load(tt):
        p = tt % 4
        SQ("dma_start", x_ring[p][:], x_d[tt * 128:(tt + 1) * 128, :],
           w=[("a_xn", tt - 3)], i=(f"axld{p}", 16))

    def a_head(tt):
        p = tt % 4
        VE("reduce_sum", c1_ring[p][:], x_ring[p][:], axis=AX,
           w=[(f"axld{p}", 16 * (tt // 4 + 1)), ("a_m", tt - 3)], i=("a_c1", 1))
        VE("tensor_scalar_mul", cm_ring[p][:], c1_ring[p][:], 1.0 / CD,
           w=[("a_c1", tt + 1), ("a_ss", tt - 3), ("a_mb", tt - 3)],
           i=("a_m", 1))
        VE("scalar_tensor_tensor", dump_sb[:], x_ring[p][:],
           cm_ring[p][:], x_ring[p][:], OP.subtract, OP.mult,
           accum_out=cs_ring[p][:],
           w=[("a_m", tt + 1), ("a_ss", tt), ("a_std", tt - 3)],
           i=("a_ss", 1))

    def a_sqrt(tt):
        p = tt % 4
        SE("activation", cd_ring[p][:], cs_ring[p][:], AF.Sqrt,
           bias=eps_c[:], scale=1.0 / CD,
           w=([("c_ve", 2)] if tt < 4 else []) + [("a_ss", tt + 1),
                                                  ("a_rc", tt - 3)],
           i=("a_std", 1))

    def a_tail(tt):
        p = tt % 4
        VE("reciprocal", cr_ring[p][:], cd_ring[p][:],
           w=[("a_std", tt + 1), ("a_xn", tt - 3), ("a_mb", tt - 3)],
           i=("a_rc", 1))
        VE("tensor_scalar", mb_ring[p][:], cm_ring[p][:],
           cr_ring[p][:], -1.0, OP.mult, OP.mult,
           w=[("a_rc", tt + 1), ("a_xn", tt - 3)], i=("a_mb", 1))

    def a_xn(tt):
        p = tt % 4
        SE("activation", xnb_ring[p][:], x_ring[p][:], AF.Identity,
           bias=mb_ring[p][:], scale=cr_ring[p][:],
           w=[("a_mb", tt + 1), (f"axnst{p}", 16 * (tt // 4))], i=("a_xn", 1))
        SE("dma_start", out=xn_scr[tt * 128:(tt + 1) * 128, :],
           in_=xnb_ring[p][:], w=[("a_xn", tt + 1)], i=(f"axnst{p}", 16))

    for tt in range(NT):
        a_load(tt)
    a_head(0)
    a_head(1)
    for tt in range(NT):
        a_sqrt(tt)
        a_tail(tt)
        a_xn(tt)
        if tt + 2 < NT:
            a_head(tt + 2)
    AB = [(f"axnst{p}", 64) for p in range(4)]
    for cc in range(NCC):
        SQ("dma_start", xnT[:, cc * T:(cc + 1) * T],
           xn_scr[:, cc * 128:(cc + 1) * 128],
           w=(AB if cc == 0 else []),
           i=("a_xnT", 16), transpose=True)
    A_XNT = kb.n("a_xnT")
    pa.close()

    if phases < 2:
        _tail(kb, [("a_xnT", A_XNT)])
        pxn.close()
        return _finish(kb, nc)

    pb1 = ExitStack()
    silu_ring = [kb.sb(f"sil_{r}", [128, CD], BF16, pb1) for r in range(2)]
    h_ring = [kb.sb(f"h_{r}", [128, CD], BF16, pb1) for r in range(2)]
    dwi_ring = [kb.sb(f"dwi_{r}", [128, 2048], I32, pb1) for r in range(2)]
    dwb_ring = [kb.sb(f"dwb_{r}", [128, 2048], BF16, pb1) for r in range(2)]
    g_ps = kb.ps("g_ps", [128, CD], pb1)
    u_ps = kb.ps("u_ps", [128, CD], pb1)

    A_DONE = [("a_xnT", A_XNT)]

    def emit_slab_fill(k):
        sg, m = k // 2, k % 2
        r = k % 4
        wsrc = wg_s if m == 0 else wu_s
        for cc in range(NCC):
            w = []
            if cc == 0:
                w.append((f"stgu{sg // 2}", 32))
                if k >= 4:
                    w.append(("b1_gmm" if m == 0 else "b1_umm", 4 * sg - 4))
            SQ("dma_start", gu_ring[r][:, cc * 512:(cc + 1) * 512],
               wsrc[sg * 512:(sg + 1) * 512, cc * 128:(cc + 1) * 128],
               w=w, i=(f"guls{r}", 16), transpose=True)

    def emit_dw_load(idx):
        if idx >= 64:
            return
        oc, qq = idx // 4, idx % 4
        p = idx % 2
        w = []
        if idx < 2:
            w = list(A_DONE)
        else:
            w = [("b1_wddq", idx - 1)]
        SQ("dma_start", dwi_ring[p][:],
           dw_d[oc * 128:(oc + 1) * 128, qq * 2048:(qq + 1) * 2048],
           w=w, i=(f"dwld{p}", 16))

    def emit_dw_compute(idx):
        oc = idx // 4
        p = idx % 2
        SE("activation", dwb_ring[p][:], dwi_ring[p][:], AF.Identity,
           bias=ds_negb[:, oc:oc + 1], scale=ds_col[:, oc:oc + 1],
           w=[(f"dwld{p}", 16 * (idx // 2 + 1)), ("c_ve", 3),
              (f"wdst{p}", 16 * (idx // 2))],
           i=("b1_wddq", 1))
        oc2, qq2 = idx // 4, idx % 4
        GQ("dma_start",
           wd_s[oc2 * 128:(oc2 + 1) * 128, qq2 * 2048:(qq2 + 1) * 2048],
           dwb_ring[p][:], w=[("b1_wddq", idx + 1)], i=(f"wdst{p}", 16))

    def emit_attn_casts():
        for q in range(4):
            r0 = q * 512
            GQ("dma_start", wq_s[r0:r0 + 512, :], wq_d[r0:r0 + 512, :],
               i=("stgq", 16))
            GQ("dma_start", wp_s[r0:r0 + 512, :], wp_d[r0:r0 + 512, :],
               i=("stgp", 16))
        GQ("dma_start", wk_s, wk_d, i=("stgk", 16))
        GQ("dma_start", wv_s, wv_d, i=("stgv", 16))

    emit_slab_fill(0)
    emit_slab_fill(1)
    emit_dw_load(0)
    emit_dw_load(1)

    for ic in range(NIC):
        sg = ic // 4
        io = ic % 4
        if io == 0:
            k = 2 * sg
            if sg + 1 < 16:
                emit_slab_fill(k + 2)
                emit_slab_fill(k + 3)
            if sg in (2, 4, 6, 8):
                emit_gu_cast(4 + (sg - 2) // 2)
            if sg == 10:
                emit_attn_casts()
        gslab = gu_ring[(2 * sg) % 4]
        uslab = gu_ring[(2 * sg + 1) % 4]
        gfills = sg + 1
        w = [(f"guls{(2 * sg) % 4}", 16 * 16 * ((2 * sg) // 4 + 1)),
             ("b1_sil", ic)]
        if ic == 0:
            w += A_DONE
        first = True
        for cc in range(NCC):
            for jc in range(4):
                TE("matmul", g_ps[:, jc * 512:(jc + 1) * 512],
                   gslab[:, cc * 512 + io * 128: cc * 512 + io * 128 + 128],
                   xnT[:, cc * T + jc * 512: cc * T + (jc + 1) * 512],
                   start=(cc == 0), stop=(cc == NCC - 1),
                   w=(w if first else ()),
                   i=(("b1_gmm", 1) if (cc == NCC - 1 and jc == 3) else None))
                first = False
        w = [(f"guls{(2 * sg + 1) % 4}", 16 * 16 * ((2 * sg + 1) // 4 + 1)),
             ("b1_h", ic)]
        if ic == 0:
            w += A_DONE
        first = True
        for cc in range(NCC):
            for jc in range(4):
                TE("matmul", u_ps[:, jc * 512:(jc + 1) * 512],
                   uslab[:, cc * 512 + io * 128: cc * 512 + io * 128 + 128],
                   xnT[:, cc * T + jc * 512: cc * T + (jc + 1) * 512],
                   start=(cc == 0), stop=(cc == NCC - 1),
                   w=(w if first else ()),
                   i=(("b1_umm", 1) if (cc == NCC - 1 and jc == 3) else None))
                first = False
        SE("activation", silu_ring[ic % 2][:], g_ps[:], AF.Silu,
           scale=gs_col[:, ic:ic + 1],
           w=[("b1_gmm", ic + 1), ("b1_h", ic - 1)] +
             ([("c_ld", C_LD)] if ic == 0 else []),
           i=("b1_sil", 1))
        emit_dw_compute(ic)
        emit_dw_load(ic + 2)
        VE("scalar_tensor_tensor", h_ring[ic % 2][:], u_ps[:],
           us_col[:, ic:ic + 1], silu_ring[ic % 2][:], OP.mult, OP.mult,
           w=[("b1_umm", ic + 1), ("b1_sil", ic + 1),
              (f"hst{ic % 2}", 16 * (ic // 2))] +
             ([("c_ld", C_LD)] if ic == 0 else []),
           i=("b1_h", 1))
        GQ("dma_start", h_scr[ic * 128:(ic + 1) * 128, :], h_ring[ic % 2][:],
           w=[("b1_h", ic + 1)], i=(f"hst{ic % 2}", 16))
    pb1.close()

    B1_DONE = [("hst0", 512), ("hst1", 512), ("wdst0", 512), ("wdst1", 512)]

    if phases < 3:
        _tail(kb, B1_DONE)
        pxn.close()
        return _finish(kb, nc)

    pgu.close()
    pkv = ExitStack()
    KT = kb.sb("KT", [128, NKV * T], BF16, pkv, side="right")
    V_sb = kb.sb("V_sb", [128, NT * 512], BF16, pkv, side="right")
    pc1 = ExitStack()
    wsl_ring = [kb.sb(f"wsl_{r}", [128, 16 * 512], BF16, pc1) for r in range(2)]
    tmpA = kb.sb("tmpA", [128, T], F32, pc1)
    tmpB = kb.sb("tmpB", [128, T], F32, pc1)
    qst_ring = [kb.sb(f"qst_{r}", [128, T], BF16, pc1) for r in range(2)]
    kq_ps = [kb.ps(f"kq_ps{r}", [128, CD], pc1) for r in range(2)]

    def c1_fill(slot, wsrc, rows0, extra_w):
        for cc in range(NCC):
            w = list(extra_w) if cc == 0 else []
            SQ("dma_start", wsl_ring[slot][:, cc * 512:(cc + 1) * 512],
               wsrc[rows0:rows0 + 512, cc * 128:(cc + 1) * 128],
               w=w, i=(f"wsl{slot}", 16), transpose=True)

    c1_fill(0, wk_s, 0, B1_DONE + [("stgk", 16)])
    c1_fill(1, wv_s, 0, B1_DONE + [("stgv", 16)])

    def rope(ps, dst_lo, dst_hi, ps_w, dst_w):
        base = kb.n("ro")
        w0 = list(ps_w) + [("c_ld", C_LD), ("ro", base)]
        VE("tensor_tensor", tmpA[0:64, :], ps[0:64, :], cos_b[0:64, :],
           OP.mult, w=w0, i=("ro", 1))
        VE("tensor_tensor", tmpB[0:64, :], ps[64:128, :], sin_b[64:128, :],
           OP.mult, w=w0, i=("ro", 1))
        VE("tensor_tensor", tmpA[64:128, :], ps[64:128, :], cos_b[64:128, :],
           OP.mult, w=w0, i=("ro", 1))
        VE("tensor_tensor", tmpB[64:128, :], ps[0:64, :], sin_b[0:64, :],
           OP.mult, w=w0, i=("ro", 1))
        wa = [("ro", base + 4)] + list(dst_w)
        VE("tensor_tensor", dst_lo, tmpA[0:64, :], tmpB[0:64, :], OP.subtract,
           w=wa, i=("ro", 1))
        VE("tensor_tensor", dst_hi, tmpA[64:128, :], tmpB[64:128, :], OP.add,
           w=wa, i=("ro", 1))

    for hk in range(NKV):
        p = hk % 2
        w = [("wsl0", 256)]
        if hk >= 2:
            w.append(("ro", 6 * (hk - 1)))
        first = True
        for cc in range(NCC):
            for jc in range(4):
                TE("matmul", kq_ps[p][:, jc * 512:(jc + 1) * 512],
                   wsl_ring[0][:, cc * 512 + hk * 128: cc * 512 + hk * 128 + 128],
                   xnT[:, cc * T + jc * 512: cc * T + (jc + 1) * 512],
                   start=(cc == 0), stop=(cc == NCC - 1),
                   w=(w if first else ()),
                   i=(("c1_kmm", 1) if (cc == NCC - 1 and jc == 3) else None))
                first = False
        rope(kq_ps[p], KT[0:64, hk * T:(hk + 1) * T],
             KT[64:128, hk * T:(hk + 1) * T], [("c1_kmm", hk + 1)], [])

    for tt in range(NT):
        p = tt % 2
        if tt == 0:
            w = [("wsl1", 256), ("ro", 18)]
        elif tt == 1:
            w = [("ro", 24)]
        else:
            w = [("c1_vev", tt - 1)]
        first = True
        for cc in range(NCC):
            TE("matmul", kq_ps[p][:, 0:512],
               xnT[:, cc * T + tt * 128: cc * T + tt * 128 + 128],
               wsl_ring[1][:, cc * 512:(cc + 1) * 512],
               start=(cc == 0), stop=(cc == NCC - 1),
               w=(w if first else ()),
               i=(("c1_vmm", 1) if cc == NCC - 1 else None))
            first = False
        SE("activation", V_sb[:, tt * 512:(tt + 1) * 512], kq_ps[p][:, 0:512],
           AF.Identity, w=[("c1_vmm", tt + 1)], i=("c1_vev", 1))

    for osl in range(4):
        slot = osl % 2
        if osl == 0:
            ew = [("stgq", 64), ("c1_kmm", 4)]
        elif osl == 1:
            ew = [("c1_vmm", 16)]
        else:
            ew = [("c1_qmm", 4 * (osl - 2) + 4)]
        c1_fill(slot, wq_s, osl * 512, ew)
        for hh in range(4):
            hq = osl * 4 + hh
            p = hq % 2
            w = [(f"wsl{slot}", 16 * 16 * (osl // 2 + 2))]
            if hq == 0:
                w.append(("c1_vev", 15))
            elif hq == 1:
                w.append(("c1_vev", 16))
            else:
                w.append(("ro", 6 * (hq + 3)))
            first = True
            for cc in range(NCC):
                for jc in range(4):
                    TE("matmul", kq_ps[p][:, jc * 512:(jc + 1) * 512],
                       wsl_ring[slot][:, cc * 512 + hh * 128:
                                      cc * 512 + hh * 128 + 128],
                       xnT[:, cc * T + jc * 512: cc * T + (jc + 1) * 512],
                       start=(cc == 0), stop=(cc == NCC - 1),
                       w=(w if first else ()),
                       i=(("c1_qmm", 1) if (cc == NCC - 1 and jc == 3) else None))
                    first = False
            rope(kq_ps[p], qst_ring[hq % 2][0:64, :], qst_ring[hq % 2][64:128, :],
                 [("c1_qmm", hq + 1)],
                 [(f"qstst{hq % 2}", 16 * (hq // 2))])
            GQ("dma_start", qT_scr[hq * 128:(hq + 1) * 128, :],
               qst_ring[hq % 2][:], w=[("ro", 6 * (hq + 5))],
               i=(f"qstst{hq % 2}", 16))
    pc1.close()
    pxn.close()
    ptab.close()

    C1_DONE = [("qstst0", 128), ("qstst1", 128)]

    if phases < 4:
        _tail(kb, C1_DONE)
        pkv.close()
        return _finish(kb, nc)

    pyt = ExitStack()
    YT = kb.sb("YT", [128, NQ * T], BF16, pyt)
    wp_sl = [kb.sb(f"wp_{q}", [128, 16 * 512], BF16, pyt) for q in range(4)]
    pc2 = ExitStack()
    qr_ring = [kb.sb(f"qr_{r}", [128, T], BF16, pc2) for r in range(3)]
    E_ring = [kb.sb(f"E_{r}", [128, 512], BF16, pc2) for r in range(4)]
    et_ring = [kb.sb(f"et_{r}", [128, 128], BF16, pc2) for r in range(2)]
    rc_ring = [kb.sb(f"rc_{r}", [128, 512], F32, pc2) for r in range(2)]
    s_ps = [kb.ps(f"s_ps{r}", [128, 512], pc2) for r in range(3)]
    d_ps = [kb.ps(f"d_ps{r}", [128, 512], pc2) for r in range(2)]
    o_ps = [kb.ps(f"o_ps{r}", [128, 512], pc2) for r in range(2)]

    groups = []
    njc_of = []
    njc = 0
    for hq in range(NQ):
        for jc in range(4):
            njs = 4 * jc + 4
            for js in range(njs):
                groups.append((hq, jc, js, njc, njs))
            njc += 1
    NG = len(groups)
    LAG = 2

    smm_after = {}
    exp_done_at = {}
    et_last_use = [0, 0]

    def c2_emit_s(j):
        hq, jc, js, njcj, njs = groups[j]
        kv = hq // 4
        if jc == 0 and js == 0:
            w = list(C1_DONE) if hq == 0 else []
            if hq >= 3:
                w.append(("c2_smm", 40 * (hq - 2)))
            SQ("dma_start", qr_ring[hq % 3][:],
               qT_scr[hq * 128:(hq + 1) * 128, :],
               w=w, i=(f"qrld{hq % 3}", 16))
            if 4 <= hq <= 7:
                ph = hq - 4
                for hd in range(NCC):
                    w2 = []
                    if ph == 0 and hd == 0:
                        w2 = C1_DONE + [("stgp", 64)]
                    SQ("dma_start", wp_sl[ph][:, hd * 512:(hd + 1) * 512],
                       wp_s[ph * 512:(ph + 1) * 512, hd * 128:(hd + 1) * 128],
                       w=w2, i=(f"wpsl{ph}", 16), transpose=True)
        w = [(f"qrld{hq % 3}", 16 * (hq // 3 + 1))]
        if j >= 3:
            pe = exp_done_at[j - 3]
            w.append(("c2_eA", pe[0]))
            if pe[1]:
                w.append(("c2_eV", pe[1]))
        if j == 0:
            w += [("ro", 24), ("c1_vev", NT)]
        TE("matmul", s_ps[j % 3][:],
           KT[:, kv * T + js * 128: kv * T + js * 128 + 128],
           qr_ring[hq % 3][:, jc * 512:(jc + 1) * 512],
           start=True, stop=True, w=w, i=("c2_smm", 1))
        smm_after[j] = kb.n("c2_smm")

    def c2_emit_exp(j):
        hq, jc, js, njcj, njs = groups[j]
        sp = j % 3
        r = js - 4 * jc
        eslot = E_ring[j % 4]
        we = [("c2_smm", smm_after[j]), ("c2_eo", j - 3), ("c2_dA", j - 3)]
        if j == 0:
            we.append(("c_ld", C_LD))
        if r < 0:
            SE("activation", eslot[:], s_ps[sp][:], AF.Exp,
               scale=SC, w=we, i=("c2_eA", 1))
        else:
            if r > 0:
                VE("memset", eslot[:, 0:128 * r], 0.0,
                   w=we, i=("c2_eV", 1))
            SE("activation", et_ring[j % 2][:],
               s_ps[sp][:, 128 * r:128 * r + 128], AF.Exp, scale=SC,
               w=we + ([("c2_eV", et_last_use[j % 2])]
                       if et_last_use[j % 2] else []),
               i=("c2_eA", 1))
            if r < 3:
                SE("activation", eslot[:, 128 * r + 128:512],
                   s_ps[sp][:, 128 * r + 128:512], AF.Exp, scale=SC,
                   w=we, i=("c2_eA", 1))
            VE("tensor_tensor", eslot[:, 128 * r:128 * r + 128],
               et_ring[j % 2][:], m01_bf[:], OP.mult,
               w=[("c2_eA", kb.n("c2_eA")), ("c_ld", C_LD)],
               i=("c2_eV", 1))
            et_last_use[j % 2] = kb.n("c2_eV")
        exp_done_at[j] = (kb.n("c2_eA"), kb.n("c2_eV"))

    def c2_emit_do(k):
        hq, jc, js, njck, njs = groups[k]
        kv = hq // 4
        par = njck % 2
        eslot = E_ring[k % 4]
        pe = exp_done_at[k]
        w = [("c2_eA", pe[0]), ("c_ve", 1)]
        if pe[1]:
            w.append(("c2_eV", pe[1]))
        if js == 0:
            w.append(("c2_yev", njck - 1))
        TE("matmul", d_ps[par][:], ones_bf[:], eslot[:],
           start=(js == 0), stop=(js == njs - 1), w=w, i=("c2_dA", 1))
        w = [("c2_eA", pe[0])]
        if pe[1]:
            w.append(("c2_eV", pe[1]))
        if js == 0:
            w.append(("c2_yev", njck - 1))
        TE("matmul", o_ps[par][:],
           V_sb[:, js * 512 + kv * 128: js * 512 + kv * 128 + 128],
           eslot[:], start=(js == 0), stop=(js == njs - 1), w=w,
           i=("c2_eo", 1))
        if js == njs - 1:
            VE("reciprocal", rc_ring[par][:], d_ps[par][:],
               w=[("c2_dA", kb.n("c2_dA")), ("c2_yev", njck - 1)],
               i=("c2_rc", 1))
            VE("tensor_tensor",
               YT[:, hq * T + jc * 512: hq * T + (jc + 1) * 512],
               o_ps[par][:], rc_ring[par][:], OP.mult,
               w=[("c2_eo", kb.n("c2_eo")), ("c2_rc", njck + 1)],
               i=("c2_yev", 1))

    for j in range(LAG):
        c2_emit_s(j)
        c2_emit_exp(j)
    for k in range(NG):
        if k + LAG < NG:
            c2_emit_s(k + LAG)
            c2_emit_exp(k + LAG)
        c2_emit_do(k)

    pc2.close()
    pkv.close()

    C2_DONE = [("c2_yev", 64)]

    if phases < 5:
        _tail(kb, C2_DONE + [(f"wpsl{q}", 256) for q in range(4)])
        pyt.close()
        return _finish(kb, nc)

    pc3 = ExitStack()
    sst_ring = [kb.sb(f"sst_{r}", [128, 8 * 512], BF16, pc3) for r in range(2)]
    c3_ps = kb.ps("c3_ps", [128, 4096], pc3)

    ph_idx = 0
    for oct_ in range(2):
        for ph in range(4):
            pr = ph_idx % 2
            w = [(f"wpsl{ph}", 256), ("c3_ev", ph_idx)]
            if ph_idx == 0:
                w += C2_DONE
            first = True
            for hd in range(NCC):
                for tt8 in range(8):
                    ttile = oct_ * 8 + tt8
                    TE("matmul", c3_ps[:, tt8 * 512:(tt8 + 1) * 512],
                       YT[:, hd * T + ttile * 128: hd * T + ttile * 128 + 128],
                       wp_sl[ph][:, hd * 512:(hd + 1) * 512],
                       start=(hd == 0), stop=(hd == NCC - 1),
                       w=(w if first else ()),
                       i=(("c3_mm", 1) if (hd == NCC - 1 and tt8 == 7) else None))
                    first = False
            VE("tensor_copy", sst_ring[pr][:], c3_ps[:],
               w=[("c3_mm", ph_idx + 1), (f"c3sst{pr}", 16 * (ph_idx // 2))] +
                 (C2_DONE if ph_idx < 2 else []),
               i=("c3_ev", 1))
            dst = s_scr[oct_ * 1024:(oct_ + 1) * 1024,
                        ph * 512:(ph + 1) * 512].rearrange(
                            "(tt p) c -> p tt c", p=128)
            src = sst_ring[pr][:].rearrange("p (tt c) -> p tt c", tt=8)
            GQ("dma_start", dst, src, w=[("c3_ev", ph_idx + 1)],
               i=(f"c3sst{pr}", 16))
            ph_idx += 1
    pc3.close()
    pyt.close()

    C3_DONE = [("c3sst0", 64), ("c3sst1", 64)]

    if phases < 6:
        _tail(kb, C3_DONE)
        return _finish(kb, nc)

    pb2 = ExitStack()
    hT = kb.sb("hT", [128, NIC * 1024], BF16, pb2)
    wd_ring = [kb.sb(f"wdr_{r}", [128, 512], BF16, pb2) for r in range(8)]
    sld_ring = [kb.sb(f"sld_{r}", [128, 8 * 512], BF16, pb2) for r in range(2)]
    x_ring2 = [kb.sb(f"x2_{r}", [128, 512], F32, pb2) for r in range(4)]
    g_ring = [kb.sb(f"g2_{r}", [128, 512], BF16, pb2) for r in range(4)]
    sg_ring = [kb.sb(f"sg_{r}", [128, 512], F32, pb2) for r in range(4)]
    o_ring = [kb.sb(f"o2_{r}", [128, 512], F32, pb2) for r in range(4)]
    b2_ps = kb.ps("b2_ps", [128, 4096], pb2)

    sl_idx = 0
    ph_idx = 0
    for oct_ in range(2):
        for ic in range(NIC):
            w = []
            if ic == 0:
                if oct_ == 0:
                    w = B1_DONE + C3_DONE
                else:
                    w = [("b2_mm8", 256)]
            SQ("dma_start", hT[:, ic * 1024:(ic + 1) * 1024],
               h_scr[ic * 128:(ic + 1) * 128, oct_ * 1024:(oct_ + 1) * 1024],
               w=w, i=("b2hld", 16))
        for ph in range(4):
            pr = ph_idx % 2
            w = [("b2_comb", 8 * (ph_idx - 1))]
            if ph_idx == 0:
                w += C3_DONE
            sdst = sld_ring[pr][:].rearrange("p (tt c) -> p tt c", tt=8)
            ssrc = s_scr[oct_ * 1024:(oct_ + 1) * 1024,
                         ph * 512:(ph + 1) * 512].rearrange(
                             "(tt p) c -> p tt c", p=128)
            SQ("dma_start", sdst, ssrc, w=w, i=(f"sldld{pr}", 16))
            for tt8 in range(8):
                ttile = oct_ * 8 + tt8
                k = ph_idx * 8 + tt8
                GQ("dma_start", x_ring2[k % 4][:],
                   x_d[ttile * 128:(ttile + 1) * 128, ph * 512:(ph + 1) * 512],
                   w=[("b2_comb", k - 3)], i=(f"xld2{k % 4}", 16))
            for ic in range(NIC):
                w = []
                if sl_idx == 0:
                    w = list(C3_DONE) + [("wdst0", 512), ("wdst1", 512)]
                if sl_idx >= 8:
                    w.append(("b2_mm8", sl_idx - 7))
                SQ("dma_start", wd_ring[sl_idx % 8][:],
                   wd_s[ph * 512:(ph + 1) * 512, ic * 128:(ic + 1) * 128],
                   w=w, i=(f"wdsl{sl_idx % 8}", 16), transpose=True)
                sl_idx += 1
            base_sl = ph_idx * NIC
            for ic in range(NIC):
                w = [(f"wdsl{(base_sl + ic) % 8}",
                      16 * ((base_sl + ic) // 8 + 1))]
                if ic == 0:
                    w += [("b2hld", 1024 * (oct_ + 1)),
                          ("b2_sil", 8 * ph_idx)]
                    if ph_idx == 0:
                        w.append(("c3_ev", 8))
                for tt8 in range(8):
                    TE("matmul", b2_ps[:, tt8 * 512:(tt8 + 1) * 512],
                       hT[:, ic * 1024 + tt8 * 128: ic * 1024 + tt8 * 128 + 128],
                       wd_ring[(base_sl + ic) % 8][:],
                       start=(ic == 0), stop=(ic == NIC - 1),
                       w=(w if tt8 == 0 else ()),
                       i=(("b2_mm8", 1) if tt8 == 7 else None))
                    w = []
            for tt8 in range(8):
                ttile = oct_ * 8 + tt8
                gi = tt8 % 4
                k = ph_idx * 8 + tt8
                SE("activation", g_ring[gi][:],
                   b2_ps[:, tt8 * 512:(tt8 + 1) * 512], AF.Silu,
                   w=[("b2_mm8", 64 * (ph_idx + 1)), ("b2_comb", k - 3)],
                   i=("b2_sil", 1))
                VE("tensor_tensor", sg_ring[gi][:], g_ring[gi][:],
                   sld_ring[pr][:, tt8 * 512:(tt8 + 1) * 512], OP.mult,
                   w=[("b2_sil", k + 1), (f"sldld{pr}", 16 * (ph_idx // 2 + 1))],
                   i=("b2_sg", 1))
                VE("tensor_tensor", o_ring[gi][:], sg_ring[gi][:],
                   x_ring2[gi][:], OP.add,
                   w=[("b2_sg", k + 1), (f"xld2{gi}", 16 * (k // 4 + 1)),
                      (f"outst{gi}", 16 * (k // 4))],
                   i=("b2_comb", 1))
                GQ("dma_start",
                   out_d[ttile * 128:(ttile + 1) * 128, ph * 512:(ph + 1) * 512],
                   o_ring[gi][:], w=[("b2_comb", k + 1)],
                   i=(f"outst{gi}", 16))
            ph_idx += 1
    pb2.close()

    _tail(kb, [(f"outst{r}", 16 * 16) for r in range(4)])
    return _finish(kb, nc)


def _tail(kb, fin):
    fy = [(kb.sem(s), v) for s, v in fin if v > 0]
    kb.ops["gpsimd"].append(lambda e, fy=fy: [e.wait_ge(s, v) for s, v in fy])


def _finish(kb, nc):
    with nc.Block() as block:
        @block.sync
        def _(e):
            for f in kb.ops["sync"]:
                f(e)

        @block.gpsimd
        def _(e):
            for f in kb.ops["gpsimd"]:
                f(e)

        @block.vector
        def _(e):
            for f in kb.ops["vector"]:
                f(e)

        @block.scalar
        def _(e):
            for f in kb.ops["scalar"]:
                f(e)

        @block.tensor
        def _(e):
            for f in kb.ops["tensor"]:
                f(e)

    kb.es.close()
    return kb


def _consts():
    import ml_dtypes
    inv_freq = 1.0 / (10000.0 ** (np.arange(64, dtype=np.float64) / 64.0))
    pos = np.arange(T, dtype=np.float64)
    ang = inv_freq[:, None] * pos[None, :]
    cos = np.concatenate([np.cos(ang), np.cos(ang)], axis=0)
    sin = np.concatenate([np.sin(ang), np.sin(ang)], axis=0)
    cosf = np.ascontiguousarray(cos.astype(np.float32))
    sinf = np.ascontiguousarray(sin.astype(np.float32))
    ii, jj = np.meshgrid(np.arange(128), np.arange(128), indexing="ij")
    m01 = np.ascontiguousarray((jj >= ii).astype(ml_dtypes.bfloat16))
    return cosf, sinf, m01


_NC_CACHE = [None]


def kernel(L0_attn_x, L0_attn_Wq, L0_attn_Wk, L0_attn_Wv, L0_attn_Wp,
           L0_ffn_gate_w, L0_ffn_gate_scale, L0_ffn_up_w, L0_ffn_up_scale,
           L0_ffn_down_w, L0_ffn_down_scale):
    x = np.ascontiguousarray(np.asarray(L0_attn_x, dtype=np.float32))
    cosf, sinf, m01 = _consts()
    common = {
        "wq": np.ascontiguousarray(np.asarray(L0_attn_Wq, np.float32)),
        "wk": np.ascontiguousarray(np.asarray(L0_attn_Wk, np.float32)),
        "wv": np.ascontiguousarray(np.asarray(L0_attn_Wv, np.float32)),
        "wp": np.ascontiguousarray(np.asarray(L0_attn_Wp, np.float32)),
        "gw": np.ascontiguousarray(np.asarray(L0_ffn_gate_w, np.int32)),
        "uw": np.ascontiguousarray(np.asarray(L0_ffn_up_w, np.int32)),
        "dw": np.ascontiguousarray(np.asarray(L0_ffn_down_w, np.int32)),
        "gs": np.ascontiguousarray(
            np.asarray(L0_ffn_gate_scale, np.float32).reshape(64, 128).T),
        "us": np.ascontiguousarray(
            np.asarray(L0_ffn_up_scale, np.float32).reshape(64, 128).T),
        "ds": np.ascontiguousarray(
            np.asarray(L0_ffn_down_scale, np.float32).reshape(16, 128).T),
        "cost": cosf, "sint": sinf, "m01": m01,
    }
    in_maps = [dict(common, x=np.ascontiguousarray(x[b])) for b in range(8)]
    if _NC_CACHE[0] is None:
        _NC_CACHE[0] = build_program()
    nc = _NC_CACHE[0].nc
    res = run_bass_kernel_spmd(nc, in_maps, core_ids=list(range(8)))
    return np.stack([np.asarray(res.results[b]["out"]) for b in range(8)],
                    axis=0)


if __name__ == "__main__":
    build_program()
    print("built ok")
